# version 01
# baseline (speedup 1.0000x reference)
import numpy as np
from contextlib import ExitStack

import concourse.bass as bass
import concourse.mybir as mybir
from concourse.bass_utils import run_bass_kernel_spmd

F32 = mybir.dt.float32
BF16 = mybir.dt.bfloat16
AF = mybir.ActivationFunctionType
ALU = mybir.AluOpType
AX = mybir.AxisListType

P = 128
NCORES = 8
B = 2
L = 8192
D = 1024
DIN = 2048
NS = 16
TOK = L // 4
CH = DIN // 4
EPS = 1e-5
BIG = 1.0e30

ENGS = ("sync", "gpsimd", "vector", "scalar", "tensor")


class Sem:
    def __init__(self, kb, name):
        self.h = kb.ctx.enter_context(kb.nc.semaphore(name))
        self.v = 0


class KB:

    def __init__(self, name="kb"):
        self.nc = bass.Bass("TRN2", target_bir_lowering=False, debug=False)
        self.ctx = ExitStack()
        self.ctx.__enter__()
        self.q = {e: [] for e in ENGS}
        self.name = name

    def sem(self, name):
        return Sem(self, name)

    def sbuf(self, name, shape, dtype):
        return self.ctx.enter_context(self.nc.sbuf_tensor(name, shape, dtype))

    def psum(self, name, shape, dtype=F32):
        return self.ctx.enter_context(self.nc.psum_tensor(name, shape, dtype))

    def din(self, name, shape, dtype=F32):
        return self.nc.dram_tensor(name, shape, dtype, kind="ExternalInput").ap()

    def dout(self, name, shape, dtype=F32):
        return self.nc.dram_tensor(name, shape, dtype, kind="ExternalOutput").ap()

    def dint(self, name, shape, dtype=F32):
        return self.nc.dram_tensor(name, shape, dtype)

    def step(self, eng, emit, waits=(), inc=None, inc_n=1):
        wl = []
        for w in waits:
            if isinstance(w, tuple):
                wl.append((w[0].h, w[1]))
            else:
                wl.append((w.h, w.v))
        if inc is not None:
            inc.v += inc_n

        def fn(e, wl=wl, emit=emit, inc=inc, inc_n=inc_n):
            for h, v in wl:
                if v > 0:
                    e.wait_ge(h, v)
            last = emit(e)
            if inc is not None:
                assert last is not None, "step with inc must return an instruction"
                last.then_inc(inc.h, inc_n)

        self.q[eng].append(fn)

    def dma(self, eng, out, in_, waits=(), inc=None):
        self.step(eng, lambda e: e.dma_start(out=out, in_=in_),
                  waits=waits, inc=inc, inc_n=16)

    def barrier_wait(self, eng, sem):
        v = sem.v

        def fn(e, v=v, h=sem.h):
            if v > 0:
                e.wait_ge(h, v)
        self.q[eng].append(fn)

    def setup_consts(self, values, sem):
        vals = sorted(set(float(v) for v in values))
        pool = self.sbuf("constpool", [P, max(1, len(vals))], F32)
        for i, v in enumerate(vals):
            self.step("gpsimd", lambda e, i=i, v=v: e.memset(pool[:, i:i + 1], v),
                      inc=sem)
            self.nc.const_aps.register(F32, v, pool[:, i:i + 1])

    def build(self):
        block = self.ctx.enter_context(self.nc.Block())
        for e in ENGS:
            fns = self.q[e]
            if not fns:
                continue

            def body(eng, fns=fns):
                for f in fns:
                    f(eng)

            getattr(block, e)(body)
        self.ctx.close()
        return self.nc


def ceil_div(a, b):
    return (a + b - 1) // b



def emit_ln_tile(kb, x_t, nf, scr, cols, VD, SD, out_bf=None, out_f32=None,
                 lnw_bc=None, lnb_bc=None, waits=(), nrows=128):
    r = slice(0, nrows)
    w = list(waits)
    kb.step("vector", lambda e: e.reduce_sum(cols[r, 0:1], x_t[r, 0:nf], axis=AX.X),
            waits=w + [SD], inc=VD)
    kb.step("scalar", lambda e: e.activation(out=scr[r, 0:nf], in_=x_t[r, 0:nf],
                                             func=AF.Square,
                                             accum_out=cols[r, 1:2]),
            waits=w + [VD], inc=SD)

    kb.step("vector", lambda e: e.tensor_scalar(out=cols[r, 2:3], in0=cols[r, 0:1],
                                                scalar1=1.0 / nf, scalar2=None,
                                                op0=ALU.mult),
            waits=[VD, SD], inc=VD)
    kb.step("vector", lambda e: e.tensor_mul(cols[r, 3:4], cols[r, 2:3],
                                             cols[r, 2:3]),
            waits=[VD], inc=VD)
    kb.step("vector", lambda e: e.scalar_tensor_tensor(out=cols[r, 4:5],
                                                       in0=cols[r, 1:2],
                                                       scalar=1.0 / nf,
                                                       in1=cols[r, 3:4],
                                                       op0=ALU.mult,
                                                       op1=ALU.subtract),
            waits=[VD], inc=VD)
    kb.step("scalar", lambda e: e.activation(out=cols[r, 5:6], in_=cols[r, 4:5],
                                             func=AF.Sqrt, bias=EPS),
            waits=[VD], inc=SD)
    kb.step("vector", lambda e: e.reciprocal(cols[r, 6:7], cols[r, 5:6]),
            waits=[SD], inc=VD)
    kb.step("vector", lambda e: e.tensor_mul(cols[r, 7:8], cols[r, 2:3],
                                             cols[r, 6:7]),
            waits=[VD], inc=VD)
    kb.step("vector", lambda e: e.tensor_scalar(out=cols[r, 7:8], in0=cols[r, 7:8],
                                                scalar1=-1.0, scalar2=None,
                                                op0=ALU.mult),
            waits=[VD], inc=VD)
    kb.step("vector", lambda e: e.tensor_scalar(out=scr[r, 0:nf], in0=x_t[r, 0:nf],
                                                scalar1=cols[r, 6:7],
                                                scalar2=cols[r, 7:8],
                                                op0=ALU.mult, op1=ALU.add),
            waits=[VD], inc=SD)

    def finish(e):
        src = scr
        last = None
        if lnw_bc is not None:
            e.tensor_mul(scr[r, 0:nf], scr[r, 0:nf], lnw_bc[r, 0:nf])
            if out_f32 is not None:
                e.tensor_add(out_f32[r, 0:nf], scr[r, 0:nf], lnb_bc[r, 0:nf])
                src = out_f32
            else:
                return e.tensor_add(out_bf[r, 0:nf], scr[r, 0:nf], lnb_bc[r, 0:nf])
        else:
            if out_f32 is not None:
                e.tensor_copy(out=out_f32[r, 0:nf], in_=scr[r, 0:nf])
                src = out_f32
        if out_bf is not None:
            last = e.tensor_copy(out=out_bf[r, 0:nf], in_=src[r, 0:nf])
        elif src is out_f32:
            last = e.engine_nop()
        return last
    kb.step("vector", finish, waits=[SD], inc=VD)


def emit_softplus(kb, dst, src_fn, t1, t2, VD, SD, nrows=128, ncols=None,
                  waits=(), bias_ap=None):
    c = slice(0, ncols)
    r = slice(0, nrows)
    if bias_ap is not None:
        kb.step("vector", lambda e: e.tensor_scalar(out=t1[r, c], in0=src_fn(e),
                                                    scalar1=bias_ap, scalar2=None,
                                                    op0=ALU.add),
                waits=list(waits), inc=SD)
    else:
        kb.step("vector", lambda e: e.tensor_copy(out=t1[r, c], in_=src_fn(e)),
                waits=list(waits), inc=SD)
    kb.step("scalar", lambda e: e.activation(out=t2[r, c], in_=t1[r, c], func=AF.Abs),
            waits=[SD], inc=SD)
    kb.step("scalar", lambda e: e.activation(out=t2[r, c], in_=t2[r, c], func=AF.Exp,
                                             scale=-1.0),
            waits=[SD], inc=SD)
    kb.step("scalar", lambda e: e.activation(out=dst[r, c], in_=t2[r, c], func=AF.Ln,
                                             bias=1.0),
            waits=[SD], inc=SD)

    def fin(e):
        e.tensor_relu(t2[r, c], t1[r, c])
        return e.tensor_add(dst[r, c], dst[r, c], t2[r, c])
    kb.step("vector", fin, waits=[SD], inc=VD)



def build_launch_A():
    kb = KB("A")
    TH = TOK + 3
    CHUNKS = [(i * 512, min(512, TH - i * 512)) for i in range(5)]

    xh = kb.din("xh", [TH, D], F32)
    lnw_bc = kb.din("lnw_bc", [P, D], F32)
    lnb_bc = kb.din("lnb_bc", [P, D], F32)
    w_in = kb.din("w_in", [D, 4 * D], F32)
    convw = kb.din("convw", [P, 16, 4], F32)
    convb = kb.din("convb", [P, 16], F32)
    mask3 = kb.din("mask3", [P, 3], F32)
    xpw = kb.din("xpw", [DIN, 96], F32)
    dtw = kb.din("dtw", [64, DIN], F32)
    dtb = kb.din("dtb", [P, 16], F32)
    ident = kb.din("ident", [P, P], BF16)

    u_out = kb.dout("u_out", [16, P, TOK], BF16)
    dt_out = kb.dout("dt_out", [16, P, TOK], BF16)
    zs_out = kb.dout("zs_out", [16, P, TOK], BF16)
    bc_out = kb.dout("bc_out", [32, TOK], BF16)

    w_sb = kb.sbuf("w_sb", [P, 8, 4 * D], BF16)
    xpw_sb = kb.sbuf("xpw_sb", [P, 16, 96], BF16)
    dtw_sb = kb.sbuf("dtw_sb", [64, DIN], BF16)
    dtb_sb = kb.sbuf("dtb_sb", [P, 16], F32)
    convw_sb = kb.sbuf("convw_sb", [P, 16, 4], F32)
    convb_sb = kb.sbuf("convb_sb", [P, 16], F32)
    mask3_sb = kb.sbuf("mask3_sb", [P, 3], F32)
    lnw_sb = kb.sbuf("lnw_sb", [P, D], F32)
    lnb_sb = kb.sbuf("lnb_sb", [P, D], F32)
    id_sb = kb.sbuf("id_sb", [P, P], BF16)

    x_tm = kb.sbuf("x_tm", [P, 4, D], F32)
    scr = kb.sbuf("scr", [P, D], F32)
    cols = kb.sbuf("cols", [P, 8], F32)
    xn_tm = kb.sbuf("xn_tm", [P, 4, D], BF16)
    xn_fm = kb.sbuf("xn_fm", [P, 8, 512], BF16)
    xi_ring = kb.sbuf("xi_ring", [P, 2, 16, 516], BF16)
    zs_ch = kb.sbuf("zs_ch", [P, 16, 512], BF16)
    cacc = kb.sbuf("cacc", [P, 512], F32)
    u_ch = kb.sbuf("u_ch", [P, 16, 512], BF16)
    dbc_ch = kb.sbuf("dbc_ch", [96, 512], F32)
    dbc_bf = kb.sbuf("dbc_bf", [64, 512], BF16)
    dt_f32 = kb.sbuf("dt_f32", [P, 512], F32)
    dt_t1 = kb.sbuf("dt_t1", [P, 512], F32)
    dt_t2 = kb.sbuf("dt_t2", [P, 512], F32)
    dt_bf = kb.sbuf("dt_bf", [P, 512], BF16)

    ps_mm = [kb.psum(f"ps_mm{i}", [P, 512], F32) for i in range(2)]
    ps_tr = [kb.psum(f"ps_tr{i}", [P, P], BF16) for i in range(2)]
    ps_xp = kb.psum("ps_xp", [96, 512], F32)
    ps_dt = kb.psum("ps_dt", [P, 512], F32)

    LD = kb.sem("LD")
    VD = kb.sem("VD")
    SD = kb.sem("SD")
    TR = kb.sem("TR")
    MM = kb.sem("MM")
    EV = kb.sem("EV")
    ST = kb.sem("ST")

    kb.setup_consts([EPS, 1.0, -1.0], LD)
    for k in range(8):
        kb.dma("gpsimd", w_sb[:, k, :], w_in[k * P:(k + 1) * P, :], inc=LD)
    for a in range(16):
        kb.dma("gpsimd", xpw_sb[:, a, :], xpw[a * P:(a + 1) * P, :], inc=LD)
    kb.dma("gpsimd", dtw_sb[:, :], dtw[:, :], inc=LD)
    kb.dma("sync", dtb_sb[:, :], dtb[:, :], inc=LD)
    kb.dma("sync", convw_sb[:, :, :], convw[:, :, :], inc=LD)
    kb.dma("sync", convb_sb[:, :], convb[:, :], inc=LD)
    kb.dma("sync", mask3_sb[:, :], mask3[:, :], inc=LD)
    kb.dma("sync", lnw_sb[:, :], lnw_bc[:, :], inc=LD)
    kb.dma("sync", lnb_sb[:, :], lnb_bc[:, :], inc=LD)
    kb.dma("sync", id_sb[:, :], ident[:, :], inc=LD)
    static_ld = LD.v

    def conv_and_tail(ci):
        slot = ci % 2
        for a in range(16):
            def conv(e, a=a, slot=slot):
                e.tensor_scalar(out=cacc[:, :], in0=xi_ring[:, slot, a, 0:512],
                                scalar1=convw_sb[:, a, 0:1], scalar2=None,
                                op0=ALU.mult)
                last = None
                for kk in range(1, 4):
                    last = e.scalar_tensor_tensor(
                        out=cacc[:, :], in0=xi_ring[:, slot, a, kk:kk + 512],
                        scalar=convw_sb[:, a, kk:kk + 1],
                        in1=cacc[:, :], op0=ALU.mult, op1=ALU.add)
                return last
            kb.step("vector", conv, waits=[VD, SD], inc=VD)
            kb.step("scalar", lambda e, a=a: e.activation(
                out=u_ch[:, a, :], in_=cacc[:, :], func=AF.Silu,
                bias=convb_sb[:, a:a + 1]), waits=[VD, MM, ST], inc=SD)
            kb.dma("sync", u_out[a, :, ci * 512:(ci + 1) * 512], u_ch[:, a, :],
                   waits=[SD], inc=ST)

        def xproj(e):
            last = None
            for a in range(16):
                last = e.matmul(ps_xp[:, :], xpw_sb[:, a, :], u_ch[:, a, :],
                                start=(a == 0), stop=(a == 15))
            return last
        kb.step("tensor", xproj, waits=[SD, EV], inc=MM)
        kb.step("scalar", lambda e: e.activation(out=dbc_ch[0:96, :],
                                                 in_=ps_xp[0:96, :], func=AF.Copy),
                waits=[MM, ST], inc=EV)
        kb.step("vector", lambda e: e.tensor_copy(out=dbc_bf[0:64, :],
                                                  in_=dbc_ch[0:64, :]),
                waits=[EV, MM], inc=VD)
        kb.dma("gpsimd", bc_out[:, ci * 512:(ci + 1) * 512], dbc_ch[64:96, :],
               waits=[EV], inc=ST)
        for m in range(16):
            def dtmm(e, m=m):
                return e.matmul(ps_dt[:, :], dtw_sb[:, m * P:(m + 1) * P],
                                dbc_bf[0:64, :], start=True, stop=True)
            kb.step("tensor", dtmm, waits=[VD, SD], inc=MM)
            emit_softplus(kb, dt_f32, lambda e: ps_dt[:, :], dt_t1, dt_t2,
                          VD, SD, ncols=512, waits=[MM, VD],
                          bias_ap=dtb_sb[:, m:m + 1])
            kb.step("vector", lambda e: e.tensor_copy(out=dt_bf[:, :],
                                                      in_=dt_f32[:, :]),
                    waits=[VD, ST], inc=VD)
            kb.dma("sync", dt_out[m, :, ci * 512:(ci + 1) * 512], dt_bf[:, :],
                   waits=[VD], inc=ST)

    for i, (c0, ncols) in enumerate(CHUNKS):
        ntt = ceil_div(ncols, P)
        for tt in range(ntt):
            rows = min(P, ncols - tt * P)
            kb.dma("sync", x_tm[0:rows, tt, :],
                   xh[c0 + tt * P:c0 + tt * P + rows, :],
                   waits=[(LD, static_ld), SD, VD], inc=LD)
            emit_ln_tile(kb, x_tm[:, tt, :], D, scr, cols, VD, SD,
                         out_bf=xn_tm[:, tt, :], lnw_bc=lnw_sb, lnb_bc=lnb_sb,
                         waits=[LD], nrows=rows)
            for f in range(8):
                slot = f % 2

                def tr(e, f=f, tt=tt, rows=rows, slot=slot):
                    return e.transpose(ps_tr[slot][:, 0:rows],
                                       xn_tm[0:rows, tt, f * P:(f + 1) * P],
                                       id_sb[0:rows, 0:rows])
                kb.step("tensor", tr, waits=[VD, EV], inc=TR)

                def trev(e, f=f, tt=tt, rows=rows, slot=slot):
                    return e.tensor_copy(
                        out=xn_fm[:, f, tt * P:tt * P + rows],
                        in_=ps_tr[slot][:, 0:rows])
                kb.step("vector", trev, waits=[TR], inc=EV)
        for m in range(32):
            slot = m % 2

            def mm(e, m=m, ncols=ncols, slot=slot):
                last = None
                for k in range(8):
                    last = e.matmul(ps_mm[slot][:, 0:ncols],
                                    w_sb[:, k, m * P:(m + 1) * P],
                                    xn_fm[:, k, 0:ncols],
                                    start=(k == 0), stop=(k == 7))
                return last
            kb.step("tensor", mm, waits=[EV], inc=MM)
            if m < 16:
                def evxi(e, m=m, i=i, ncols=ncols, slot=slot):
                    return e.tensor_copy(out=xi_ring[:, i % 2, m, 0:ncols],
                                         in_=ps_mm[slot][:, 0:ncols])
                kb.step("vector", evxi, waits=[MM], inc=EV)
            else:
                def evz(e, m=m, ncols=ncols, slot=slot):
                    return e.activation(out=zs_ch[:, m - 16, 0:ncols],
                                        in_=ps_mm[slot][:, 0:ncols], func=AF.Silu)
                kb.step("scalar", evz, waits=[MM, ST], inc=EV)
                lo_x, hi_x = c0, c0 + ncols
                lo_r, hi_r = max(0, lo_x - 3), min(TOK, hi_x - 3)
                if hi_r > lo_r:
                    s0 = lo_r + 3 - lo_x
                    kb.dma("sync",
                           zs_out[m - 16, :, lo_r:hi_r],
                           zs_ch[:, m - 16, s0:s0 + (hi_r - lo_r)],
                           waits=[EV], inc=ST)
        if i == 0:
            def maskh(e):
                last = None
                for a in range(16):
                    last = e.tensor_mul(xi_ring[:, 0, a, 0:3],
                                        xi_ring[:, 0, a, 0:3], mask3_sb[:, :])
                return last
            kb.step("vector", maskh, waits=[EV, LD], inc=VD)
        if i >= 1:
            def patch(e, i=i):
                last = None
                for a in range(16):
                    last = e.tensor_copy(out=xi_ring[:, (i - 1) % 2, a, 512:515],
                                         in_=xi_ring[:, i % 2, a, 0:3])
                return last
            kb.step("vector", patch, waits=[EV], inc=VD)
            conv_and_tail(i - 1)

    kb.barrier_wait("sync", ST)
    return kb.build()


def prep_launch_A(inp, consts):
    x = inp["L0_attn_x"]
    lnw_bc = np.tile(np.asarray(inp["L0_attn_ln_w"])[None, :], (P, 1)).astype(np.float32)
    lnb_bc = np.tile(np.asarray(inp["L0_attn_ln_b"])[None, :], (P, 1)).astype(np.float32)
    convw = np.ascontiguousarray(
        np.asarray(inp["L0_attn_conv_w"]).reshape(16, P, 4).transpose(1, 0, 2))
    convb = np.ascontiguousarray(np.asarray(inp["L0_attn_conv_b"]).reshape(16, P).T)
    dtb = np.ascontiguousarray(np.asarray(inp["L0_attn_dt_proj_b"]).reshape(16, P).T)
    in_maps = []
    for k in range(NCORES):
        b, t0 = k // 4, (k % 4) * TOK
        if t0 == 0:
            xhs = np.concatenate([np.zeros((3, D), np.float32), x[b, 0:TOK]], axis=0)
            mask3 = np.zeros((P, 3), np.float32)
        else:
            xhs = x[b, t0 - 3:t0 + TOK]
            mask3 = np.ones((P, 3), np.float32)
        in_maps.append(dict(
            xh=np.ascontiguousarray(xhs),
            lnw_bc=lnw_bc, lnb_bc=lnb_bc,
            w_in=np.asarray(inp["L0_attn_in_proj_w"]),
            convw=convw, convb=convb, mask3=mask3,
            xpw=np.asarray(inp["L0_attn_x_proj_w"]),
            dtw=np.asarray(inp["L0_attn_dt_proj_w"]),
            dtb=dtb,
            ident=consts["ident_bf"],
        ))
    return in_maps



def build_launch_D():
    kb = KB("D")
    TH = TOK + 6
    CHUNKS = [(i * 512, min(512, TH - i * 512)) for i in range(5)]

    xh = kb.din("xh", [TH, D], F32)
    w_in = kb.din("w_in", [D, 4 * D], F32)
    convw_f = kb.din("convw_f", [P, 16, 4], F32)
    convb_f = kb.din("convb_f", [P, 16], F32)
    convw_b = kb.din("convw_b", [P, 16, 4], F32)
    convb_b = kb.din("convb_b", [P, 16], F32)
    xpw_f = kb.din("xpw_f", [DIN, 96], F32)
    xpw_b = kb.din("xpw_b", [DIN, 96], F32)
    dtw_f = kb.din("dtw_f", [64, DIN], F32)
    dtw_b = kb.din("dtw_b", [64, DIN], F32)
    dtb_f = kb.din("dtb_f", [P, 16], F32)
    dtb_b = kb.din("dtb_b", [P, 16], F32)
    ident = kb.din("ident", [P, P], BF16)

    u_f_out = kb.dout("u_f_out", [16, P, TOK], BF16)
    dt_f_out = kb.dout("dt_f_out", [16, P, TOK], BF16)
    u_b_out = kb.dout("u_b_out", [16, P, TOK], BF16)
    dt_b_out = kb.dout("dt_b_out", [16, P, TOK], BF16)
    zs_out = kb.dout("zs_out", [16, P, TOK], BF16)
    bc_f_out = kb.dout("bc_f_out", [32, TOK], BF16)
    bc_b_out = kb.dout("bc_b_out", [32, TOK], BF16)

    w_sb = kb.sbuf("w_sb", [P, 8, 4 * D], BF16)
    xpw_sb = kb.sbuf("xpw_sb", [P, 2, 16, 96], BF16)
    dtw_sb = kb.sbuf("dtw_sb", [64, 2, DIN], BF16)
    dtb_sb = kb.sbuf("dtb_sb", [P, 2, 16], F32)
    convw_sb = kb.sbuf("convw_sb", [P, 2, 16, 4], F32)
    convb_sb = kb.sbuf("convb_sb", [P, 2, 16], F32)
    id_sb = kb.sbuf("id_sb", [P, P], BF16)

    xn_tm = kb.sbuf("xn_tm", [P, 4, D], BF16)
    xn_fm = kb.sbuf("xn_fm", [P, 8, 512], BF16)
    xi_ring = kb.sbuf("xi_ring", [P, 2, 16, 518], BF16)
    zs_ch = kb.sbuf("zs_ch", [P, 16, 512], BF16)
    cacc = kb.sbuf("cacc", [P, 512], F32)
    u_ch = kb.sbuf("u_ch", [P, 16, 512], BF16)
    dbc_ch = kb.sbuf("dbc_ch", [96, 512], F32)
    dbc_bf = kb.sbuf("dbc_bf", [64, 512], BF16)
    dt_f32 = kb.sbuf("dt_f32", [P, 512], F32)
    dt_t1 = kb.sbuf("dt_t1", [P, 512], F32)
    dt_t2 = kb.sbuf("dt_t2", [P, 512], F32)
    dt_bf = kb.sbuf("dt_bf", [P, 512], BF16)

    ps_mm = [kb.psum(f"ps_mm{i}", [P, 512], F32) for i in range(2)]
    ps_tr = [kb.psum(f"ps_tr{i}", [P, P], BF16) for i in range(2)]
    ps_xp = kb.psum("ps_xp", [96, 512], F32)
    ps_dt = kb.psum("ps_dt", [P, 512], F32)

    LD = kb.sem("LD")
    VD = kb.sem("VD")
    SD = kb.sem("SD")
    TR = kb.sem("TR")
    MM = kb.sem("MM")
    EV = kb.sem("EV")
    ST = kb.sem("ST")

    kb.setup_consts([EPS, 1.0, -1.0], LD)
    for k in range(8):
        kb.dma("gpsimd", w_sb[:, k, :], w_in[k * P:(k + 1) * P, :], inc=LD)
    for d_i, (xpw, dtw, dtb, cw, cb) in enumerate(
            [(xpw_f, dtw_f, dtb_f, convw_f, convb_f),
             (xpw_b, dtw_b, dtb_b, convw_b, convb_b)]):
        for a in range(16):
            kb.dma("gpsimd", xpw_sb[:, d_i, a, :], xpw[a * P:(a + 1) * P, :], inc=LD)
        kb.dma("gpsimd", dtw_sb[:, d_i, :], dtw[:, :], inc=LD)
        kb.dma("sync", dtb_sb[:, d_i, :], dtb[:, :], inc=LD)
        kb.dma("sync", convw_sb[:, d_i, :, :], cw[:, :, :], inc=LD)
        kb.dma("sync", convb_sb[:, d_i, :], cb[:, :], inc=LD)
    kb.dma("sync", id_sb[:, :], ident[:, :], inc=LD)
    static_ld = LD.v

    def conv_and_tail(ci):
        slot = ci % 2
        for d_i, (shifts, u_out, dt_out, bc_out) in enumerate(
                [((0, 1, 2, 3), u_f_out, dt_f_out, bc_f_out),
                 ((6, 5, 4, 3), u_b_out, dt_b_out, bc_b_out)]):
            for a in range(16):
                def conv(e, a=a, slot=slot, d_i=d_i, shifts=shifts):
                    e.tensor_scalar(out=cacc[:, :],
                                    in0=xi_ring[:, slot, a, shifts[0]:shifts[0] + 512],
                                    scalar1=convw_sb[:, d_i, a, 0:1], scalar2=None,
                                    op0=ALU.mult)
                    last = None
                    for kk in range(1, 4):
                        last = e.scalar_tensor_tensor(
                            out=cacc[:, :],
                            in0=xi_ring[:, slot, a, shifts[kk]:shifts[kk] + 512],
                            scalar=convw_sb[:, d_i, a, kk:kk + 1],
                            in1=cacc[:, :], op0=ALU.mult, op1=ALU.add)
                    return last
                kb.step("vector", conv, waits=[VD, SD], inc=VD)
                kb.step("scalar", lambda e, a=a, d_i=d_i: e.activation(
                    out=u_ch[:, a, :], in_=cacc[:, :], func=AF.Silu,
                    bias=convb_sb[:, d_i, a:a + 1]), waits=[VD, MM, ST], inc=SD)
                kb.dma("sync", u_out[a, :, ci * 512:(ci + 1) * 512], u_ch[:, a, :],
                       waits=[SD], inc=ST)

            def xproj(e, d_i=d_i):
                last = None
                for a in range(16):
                    last = e.matmul(ps_xp[:, :], xpw_sb[:, d_i, a, :], u_ch[:, a, :],
                                    start=(a == 0), stop=(a == 15))
                return last
            kb.step("tensor", xproj, waits=[SD, EV], inc=MM)
            kb.step("scalar", lambda e: e.activation(out=dbc_ch[0:96, :],
                                                     in_=ps_xp[0:96, :], func=AF.Copy),
                    waits=[MM, ST], inc=EV)
            kb.step("vector", lambda e: e.tensor_copy(out=dbc_bf[0:64, :],
                                                      in_=dbc_ch[0:64, :]),
                    waits=[EV, MM], inc=VD)
            kb.dma("gpsimd", bc_out[:, ci * 512:(ci + 1) * 512], dbc_ch[64:96, :],
                   waits=[EV], inc=ST)
            for m in range(16):
                def dtmm(e, m=m, d_i=d_i):
                    return e.matmul(ps_dt[:, :], dtw_sb[:, d_i, m * P:(m + 1) * P],
                                    dbc_bf[0:64, :], start=True, stop=True)
                kb.step("tensor", dtmm, waits=[VD, SD], inc=MM)
                emit_softplus(kb, dt_f32, lambda e: ps_dt[:, :], dt_t1, dt_t2,
                              VD, SD, ncols=512, waits=[MM, VD],
                              bias_ap=dtb_sb[:, d_i, m:m + 1])
                kb.step("vector", lambda e: e.tensor_copy(out=dt_bf[:, :],
                                                          in_=dt_f32[:, :]),
                        waits=[VD, ST], inc=VD)
                kb.dma("sync", dt_out[m, :, ci * 512:(ci + 1) * 512], dt_bf[:, :],
                       waits=[VD], inc=ST)

    for i, (c0, ncols) in enumerate(CHUNKS):
        ntt = ceil_div(ncols, P)
        for tt in range(ntt):
            rows = min(P, ncols - tt * P)
            kb.dma("gpsimd", xn_tm[0:rows, tt, :],
                   xh[c0 + tt * P:c0 + tt * P + rows, :],
                   waits=[(LD, static_ld), TR], inc=LD)
            for f in range(8):
                slot = f % 2

                def tr(e, f=f, tt=tt, rows=rows, slot=slot):
                    return e.transpose(ps_tr[slot][:, 0:rows],
                                       xn_tm[0:rows, tt, f * P:(f + 1) * P],
                                       id_sb[0:rows, 0:rows])
                kb.step("tensor", tr, waits=[LD, EV], inc=TR)

                def trev(e, f=f, tt=tt, rows=rows, slot=slot):
                    return e.tensor_copy(
                        out=xn_fm[:, f, tt * P:tt * P + rows],
                        in_=ps_tr[slot][:, 0:rows])
                kb.step("vector", trev, waits=[TR], inc=EV)
        for m in range(32):
            slot = m % 2

            def mm(e, m=m, ncols=ncols, slot=slot):
                last = None
                for k in range(8):
                    last = e.matmul(ps_mm[slot][:, 0:ncols],
                                    w_sb[:, k, m * P:(m + 1) * P],
                                    xn_fm[:, k, 0:ncols],
                                    start=(k == 0), stop=(k == 7))
                return last
            kb.step("tensor", mm, waits=[EV], inc=MM)
            if m < 16:
                def evxi(e, m=m, i=i, ncols=ncols, slot=slot):
                    return e.tensor_copy(out=xi_ring[:, i % 2, m, 0:ncols],
                                         in_=ps_mm[slot][:, 0:ncols])
                kb.step("vector", evxi, waits=[MM], inc=EV)
            else:
                def evz(e, m=m, ncols=ncols, slot=slot):
                    return e.activation(out=zs_ch[:, m - 16, 0:ncols],
                                        in_=ps_mm[slot][:, 0:ncols], func=AF.Silu)
                kb.step("scalar", evz, waits=[MM, ST], inc=EV)
                lo_x, hi_x = c0, c0 + ncols
                lo_r, hi_r = max(0, lo_x - 3), min(TOK, hi_x - 3)
                if hi_r > lo_r:
                    s0 = lo_r + 3 - lo_x
                    kb.dma("sync",
                           zs_out[m - 16, :, lo_r:hi_r],
                           zs_ch[:, m - 16, s0:s0 + (hi_r - lo_r)],
                           waits=[EV], inc=ST)
        if i >= 1:
            def patch(e, i=i):
                last = None
                for a in range(16):
                    last = e.tensor_copy(out=xi_ring[:, (i - 1) % 2, a, 512:518],
                                         in_=xi_ring[:, i % 2, a, 0:6])
                return last
            kb.step("vector", patch, waits=[EV], inc=VD)
            conv_and_tail(i - 1)

    kb.barrier_wait("sync", ST)
    return kb.build()


def prep_launch_D(inp, consts, x2_full):
    w_inT = np.ascontiguousarray(np.asarray(inp["L1_attn_in_proj_w"]).T)
    xpw_fT = np.ascontiguousarray(np.asarray(inp["L1_attn_x_proj_w"]).T)
    xpw_bT = np.ascontiguousarray(np.asarray(inp["L1_attn_x_proj_wb"]).T)
    dtw_fT = np.ascontiguousarray(np.asarray(inp["L1_attn_dt_proj_w"]).T)
    dtw_bT = np.ascontiguousarray(np.asarray(inp["L1_attn_dt_proj_wb"]).T)

    def vec(v):
        return np.ascontiguousarray(np.asarray(v).reshape(16, P).T)

    def cwr(w):
        return np.ascontiguousarray(np.asarray(w).reshape(16, P, 4).transpose(1, 0, 2))

    in_maps = []
    for k in range(NCORES):
        b, t0 = k // 4, (k % 4) * TOK
        lo, hi = t0 - 3, t0 + TOK + 3
        pad_l, pad_r = max(0, -lo), max(0, hi - L)
        xhs = x2_full[b, max(0, lo):min(L, hi)]
        if pad_l or pad_r:
            xhs = np.concatenate([np.zeros((pad_l, D), np.float32), xhs,
                                  np.zeros((pad_r, D), np.float32)], axis=0)
        in_maps.append(dict(
            xh=np.ascontiguousarray(xhs),
            w_in=w_inT,
            convw_f=cwr(inp["L1_attn_conv_w"]), convb_f=vec(inp["L1_attn_conv_b"]),
            convw_b=cwr(inp["L1_attn_conv_wb"]), convb_b=vec(inp["L1_attn_conv_bb"]),
            xpw_f=xpw_fT, xpw_b=xpw_bT,
            dtw_f=dtw_fT, dtw_b=dtw_bT,
            dtb_f=vec(inp["L1_attn_dt_proj_b"]), dtb_b=vec(inp["L1_attn_dt_proj_bb"]),
            ident=consts["ident_bf"],
        ))
    return in_maps



def build_launch_scan(ndir=1):
    kb = KB("S")
    NSEG = 4
    SEG = L // NSEG

    ins = []
    for j in range(ndir):
        sfx = f"_d{j}"
        ins.append(dict(
            dt=kb.din("dt" + sfx, [4, P, L], BF16),
            u=kb.din("u" + sfx, [4, P, L], BF16),
            Bbc=kb.din("Bbc" + sfx, [NS, P, L], BF16),
            Cbc=kb.din("Cbc" + sfx, [NS, P, L], BF16),
            Acol=kb.din("Acol" + sfx, [P, 4, NS], F32),
            Dcol=kb.din("Dcol" + sfx, [P, 4], F32),
        ))
    zs_in = kb.din("zs", [4, P, L], BF16)
    y_out = kb.dout("y_out", [4, P, L], BF16)

    dt_sb = kb.sbuf("dt_sb", [P, L], BF16)
    u_sb = kb.sbuf("u_sb", [P, L], BF16)
    zs_sb = kb.sbuf("zs_sb", [P, SEG], BF16)
    dtu_sb = kb.sbuf("dtu_sb", [P, SEG], F32)
    Bbc_sb = kb.sbuf("Bbc_sb", [P, SEG], F32)
    Cbc_sb = kb.sbuf("Cbc_sb", [P, SEG], F32)
    a_sb = kb.sbuf("a_sb", [P, SEG], F32)
    b_sb = kb.sbuf("b_sb", [P, SEG], F32)
    h_sb = kb.sbuf("h_sb", [P, SEG], F32)
    ym_sb = kb.sbuf("ym_sb", [P, SEG], F32)
    y_sb = kb.sbuf("y_sb", [P, L], F32)
    ya_sb = kb.sbuf("ya_sb", [P, L], F32) if ndir > 1 else None
    u32_sb = kb.sbuf("u32_sb", [P, SEG], F32)
    ybf_sb = kb.sbuf("ybf_sb", [P, SEG], BF16)
    An_sb = kb.sbuf("An_sb", [P, ndir * 4 * NS], F32)
    Dc_sb = kb.sbuf("Dc_sb", [P, ndir * 4], F32)
    carry = kb.sbuf("carry", [P, NS], F32)

    LD = kb.sem("LD")
    VD = kb.sem("VD")
    SD = kb.sem("SD")
    ST = kb.sem("ST")

    kb.setup_consts([EPS, 1.0, -1.0], LD)

    for j in range(ndir):
        kb.dma("sync", An_sb[:, j * 64:(j + 1) * 64],
               ins[j]["Acol"].rearrange("p a s -> p (a s)"), inc=LD)
        kb.dma("sync", Dc_sb[:, j * 4:(j + 1) * 4], ins[j]["Dcol"][:, :], inc=LD)
    kb.step("scalar", lambda e: e.activation(out=An_sb[:, 0:ndir * 64],
                                             in_=An_sb[:, 0:ndir * 64],
                                             func=AF.Exp),
            waits=[LD], inc=SD)
    kb.step("vector", lambda e: e.tensor_scalar(out=An_sb[:, 0:ndir * 64],
                                                in0=An_sb[:, 0:ndir * 64],
                                                scalar1=-1.0, scalar2=None,
                                                op0=ALU.mult),
            waits=[SD], inc=VD)

    for a in range(4):
        for j in range(ndir):
            rev = (j == 1)
            kb.dma("gpsimd", dt_sb[:, :], ins[j]["dt"][a, :, :],
                   waits=[SD, VD], inc=LD)
            kb.dma("gpsimd", u_sb[:, :], ins[j]["u"][a, :, :],
                   waits=[SD, VD], inc=LD)
            segs = range(NSEG - 1, -1, -1) if rev else range(NSEG)
            for si, sg in enumerate(segs):
                gs = slice(sg * SEG, (sg + 1) * SEG)
                kb.step("vector", lambda e, gs=gs: e.tensor_mul(
                    dtu_sb[:, :], dt_sb[:, gs], u_sb[:, gs]),
                    waits=[LD, VD, SD], inc=VD)
                for s in range(NS):
                    kb.dma("gpsimd", Bbc_sb[:, :], ins[j]["Bbc"][s, :, gs],
                           waits=[VD], inc=LD)
                    kb.dma("gpsimd", Cbc_sb[:, :], ins[j]["Cbc"][s, :, gs],
                           waits=[VD], inc=LD)
                    kb.step("scalar", lambda e, a=a, j=j, s=s, gs=gs: e.activation(
                        out=a_sb[:, :], in_=dt_sb[:, gs], func=AF.Exp,
                        scale=An_sb[:, j * 64 + a * NS + s:j * 64 + a * NS + s + 1]),
                        waits=[VD, LD], inc=SD)
                    kb.step("vector", lambda e: e.tensor_mul(
                        b_sb[:, :], dtu_sb[:, :], Bbc_sb[:, :]),
                        waits=[LD, SD], inc=VD)

                    def scan(e, si=si, s=s, rev=rev):
                        init = 0.0 if si == 0 else carry[:, s:s + 1]
                        if rev:
                            return e.tensor_tensor_scan(
                                out=h_sb[:, ::-1], data0=a_sb[:, ::-1],
                                data1=b_sb[:, ::-1],
                                initial=init, op0=ALU.mult, op1=ALU.add)
                        return e.tensor_tensor_scan(
                            out=h_sb[:, :], data0=a_sb[:, :], data1=b_sb[:, :],
                            initial=init, op0=ALU.mult, op1=ALU.add)
                    kb.step("vector", scan, waits=[VD, SD], inc=VD)
                    if si < NSEG - 1:
                        cc = slice(0, 1) if rev else slice(SEG - 1, SEG)
                        kb.step("vector", lambda e, s=s, cc=cc: e.tensor_copy(
                            out=carry[:, s:s + 1], in_=h_sb[:, cc]),
                            waits=[VD], inc=VD)
                    kb.step("vector", lambda e: e.tensor_mul(
                        ym_sb[:, :], h_sb[:, :], Cbc_sb[:, :]),
                        waits=[VD], inc=VD)
                    if s == 0:
                        kb.step("vector", lambda e, gs=gs: e.tensor_copy(
                            out=y_sb[:, gs], in_=ym_sb[:, :]),
                            waits=[VD], inc=VD)
                    else:
                        kb.step("vector", lambda e, gs=gs: e.tensor_add(
                            y_sb[:, gs], y_sb[:, gs], ym_sb[:, :]),
                            waits=[VD], inc=VD)
                kb.step("vector", lambda e, gs=gs: e.tensor_copy(
                    out=u32_sb[:, :], in_=u_sb[:, gs]),
                    waits=[VD], inc=VD)
                kb.step("vector", lambda e, a=a, j=j, gs=gs: e.scalar_tensor_tensor(
                    out=y_sb[:, gs], in0=u32_sb[:, :],
                    scalar=Dc_sb[:, j * 4 + a:j * 4 + a + 1],
                    in1=y_sb[:, gs], op0=ALU.mult, op1=ALU.add),
                    waits=[VD], inc=VD)
                if ndir > 1:
                    if j == 0:
                        kb.step("vector", lambda e, gs=gs: e.tensor_copy(
                            out=ya_sb[:, gs], in_=y_sb[:, gs]),
                            waits=[VD], inc=VD)
                    else:
                        kb.step("vector", lambda e, gs=gs: e.tensor_add(
                            ya_sb[:, gs], ya_sb[:, gs], y_sb[:, gs]),
                            waits=[VD], inc=VD)
        ysrc = ya_sb if ndir > 1 else y_sb
        for sg in range(NSEG):
            gs = slice(sg * SEG, (sg + 1) * SEG)
            kb.dma("gpsimd", zs_sb[:, :], zs_in[a, :, gs], waits=[VD, ST], inc=LD)
            kb.step("vector", lambda e, gs=gs: e.tensor_mul(
                ybf_sb[:, :], ysrc[:, gs], zs_sb[:, :]),
                waits=[VD, LD, ST], inc=VD)
            kb.dma("sync", y_out[a, :, gs], ybf_sb[:, :], waits=[VD], inc=ST)

    kb.barrier_wait("sync", ST)
    return kb.build()


def prep_launch_scan(dirs, zs_full):
    in_maps = []
    for k in range(NCORES):
        b, c0 = k // 4, (k % 4) * CH
        m = {}
        for j, dd in enumerate(dirs):
            sfx = f"_d{j}"
            m["dt" + sfx] = np.ascontiguousarray(
                dd["dt"][b, c0:c0 + CH].reshape(4, P, L))
            m["u" + sfx] = np.ascontiguousarray(
                dd["u"][b, c0:c0 + CH].reshape(4, P, L))
            m["Bbc" + sfx] = np.ascontiguousarray(
                np.broadcast_to(dd["Bm"][b][:, None, :], (NS, P, L)))
            m["Cbc" + sfx] = np.ascontiguousarray(
                np.broadcast_to(dd["Cm"][b][:, None, :], (NS, P, L)))
            m["Acol" + sfx] = np.ascontiguousarray(
                np.asarray(dd["A_log"])[c0:c0 + CH].reshape(4, P, NS)
                .transpose(1, 0, 2)).astype(np.float32)
            m["Dcol" + sfx] = np.ascontiguousarray(
                np.asarray(dd["D"])[c0:c0 + CH].reshape(4, P).T).astype(np.float32)
        m["zs"] = np.ascontiguousarray(zs_full[b, c0:c0 + CH].reshape(4, P, L))
        in_maps.append(m)
    return in_maps


def assemble_scan_y(res):
    out = np.zeros((B, DIN, L), dtype=res[0]["y_out"].dtype)
    for k in range(NCORES):
        b, c0 = k // 4, (k % 4) * CH
        out[b, c0:c0 + CH] = res[k]["y_out"].reshape(CH, L)
    return out



def emit_gate_top2(kb, glog, gcol, gtmp, comb_slice, VD, SD, waits=()):
    NE = 8
    kb.step("vector", lambda e: e.reduce_max(gcol[:, 0:1], glog[:, 0:NE], axis=AX.X),
            waits=list(waits) + [VD, SD], inc=VD)
    kb.step("vector", lambda e: e.tensor_scalar(out=comb_slice, in0=glog[:, 0:NE],
                                                scalar1=gcol[:, 0:1], scalar2=None,
                                                op0=ALU.is_equal),
            waits=[VD], inc=VD)
    kb.step("vector", lambda e: e.tensor_scalar(out=gtmp[:, 0:NE], in0=comb_slice,
                                                scalar1=-BIG, scalar2=None,
                                                op0=ALU.mult),
            waits=[VD], inc=VD)
    kb.step("vector", lambda e: e.tensor_add(gtmp[:, 0:NE], gtmp[:, 0:NE],
                                             glog[:, 0:NE]),
            waits=[VD], inc=VD)
    kb.step("vector", lambda e: e.reduce_max(gcol[:, 1:2], gtmp[:, 0:NE], axis=AX.X),
            waits=[VD], inc=VD)
    kb.step("vector", lambda e: e.tensor_scalar(out=gtmp[:, 0:NE], in0=gtmp[:, 0:NE],
                                                scalar1=gcol[:, 1:2], scalar2=None,
                                                op0=ALU.is_equal),
            waits=[VD], inc=VD)
    kb.step("vector", lambda e: e.tensor_sub(gcol[:, 2:3], gcol[:, 1:2],
                                             gcol[:, 0:1]),
            waits=[VD], inc=VD)
    kb.step("scalar", lambda e: e.activation(out=gcol[:, 3:4], in_=gcol[:, 2:3],
                                             func=AF.Exp),
            waits=[VD], inc=SD)
    kb.step("vector", lambda e: e.tensor_scalar(out=gcol[:, 3:4], in0=gcol[:, 3:4],
                                                scalar1=1.0, scalar2=None,
                                                op0=ALU.add),
            waits=[SD], inc=VD)
    kb.step("vector", lambda e: e.reciprocal(gcol[:, 4:5], gcol[:, 3:4]),
            waits=[VD], inc=VD)
    kb.step("vector", lambda e: e.tensor_scalar(out=gcol[:, 5:6], in0=gcol[:, 4:5],
                                                scalar1=-1.0, scalar2=1.0,
                                                op0=ALU.mult, op1=ALU.add),
            waits=[VD], inc=VD)
    kb.step("vector", lambda e: e.tensor_scalar(out=comb_slice, in0=comb_slice,
                                                scalar1=gcol[:, 4:5], scalar2=None,
                                                op0=ALU.mult),
            waits=[VD], inc=VD)
    kb.step("vector", lambda e: e.scalar_tensor_tensor(out=comb_slice,
                                                       in0=gtmp[:, 0:NE],
                                                       scalar=gcol[:, 5:6],
                                                       in1=comb_slice,
                                                       op0=ALU.mult, op1=ALU.add),
            waits=[VD], inc=VD)


def build_launch_C():
    kb = KB("C")
    NE, HID = 8, 1536
    NTT = TOK // P
    NCHK = 4

    y_in = kb.din("y_in", [16, P, TOK], BF16)
    x_in = kb.din("x_in", [TOK, D], F32)
    wout = kb.din("wout", [DIN, D], F32)
    wr_in = kb.din("wr_in", [D, NE], F32)
    w1_in = kb.din("w1_in", [NE, D, HID], F32)
    w2_in = kb.din("w2_in", [NE, HID, D], F32)
    w3_in = kb.din("w3_in", [NE, D, HID], F32)
    ident = kb.din("ident", [P, P], BF16)

    x1_out = kb.dout("x1_out", [TOK, D], F32)
    x2_out = kb.dout("x2_out", [TOK, D], F32)

    wbuf = kb.sbuf("wbuf", [P, 16, D], BF16)
    w13 = kb.sbuf("w13", [P, 8, HID], BF16)
    wr_sb = kb.sbuf("wr_sb", [P, 8, NE], BF16)
    id_sb = kb.sbuf("id_sb", [P, P], BF16)
    y_blk = kb.sbuf("y_blk", [P, 16, P], BF16)
    x_t = kb.sbuf("x_t", [P, D], F32)
    res_t = kb.sbuf("res_t", [P, D], F32)
    scr = kb.sbuf("scr", [P, D], F32)
    cols = kb.sbuf("cols", [P, 8], F32)
    x1_sb = kb.sbuf("x1_sb", [P, NTT, D], F32)
    x1fm = kb.sbuf("x1fm", [P, 8, 512], BF16)
    x1bf = kb.sbuf("x1bf", [P, D], BF16)
    glog_cm = kb.sbuf("glog_cm", [8, 512], BF16)
    glog = kb.sbuf("glog", [P, 8], F32)
    gcol = kb.sbuf("gcol", [P, 8], F32)
    gtmp = kb.sbuf("gtmp", [P, 8], F32)
    comb = kb.sbuf("comb", [P, NTT, NE], F32)
    hsil = kb.sbuf("hsil", [P, 12, 512], BF16)
    gbuf = kb.sbuf("gbuf", [P, 12, 512], BF16)
    yff = kb.sbuf("yff", [P, 4, D], F32)

    ps_o = [kb.psum(f"ps_o{i}", [P, 512], F32) for i in range(2)]
    ps_tr = [kb.psum(f"ps_tr{i}", [P, P], BF16) for i in range(2)]
    ps_g = kb.psum("ps_g", [8, 512], F32)
    ps_h = kb.psum("ps_h", [P, 512], F32)
    ps_y = [kb.psum(f"ps_y{i}", [P, 512], F32) for i in range(2)]

    LD = kb.sem("LD")
    VD = kb.sem("VD")
    SD = kb.sem("SD")
    TR = kb.sem("TR")
    MM = kb.sem("MM")
    EV = kb.sem("EV")
    ST = kb.sem("ST")

    kb.setup_consts([EPS, 1.0, -1.0], LD)

    for a in range(16):
        kb.dma("gpsimd", wbuf[:, a, :], wout[a * P:(a + 1) * P, :], inc=LD)
    for k in range(8):
        kb.dma("gpsimd", wr_sb[:, k, :], wr_in[k * P:(k + 1) * P, :], inc=LD)
    kb.dma("sync", id_sb[:, :], ident[:, :], inc=LD)
    static_ld = LD.v

    for m in range(NTT):
        for a in range(16):
            kb.dma("sync", y_blk[:, a, :], y_in[a, :, m * P:(m + 1) * P],
                   waits=[(LD, static_ld), MM], inc=LD)
        kb.dma("sync", x_t[:, :], x_in[m * P:(m + 1) * P, :], waits=[VD], inc=LD)

        def mmop(e):
            last = None
            for a in range(16):
                for nn in range(2):
                    last = e.matmul(ps_o[nn][:, :], y_blk[:, a, :],
                                    wbuf[:, a, nn * 512:(nn + 1) * 512],
                                    start=(a == 0), stop=(a == 15))
            return last
        kb.step("tensor", mmop, waits=[LD, EV], inc=MM)
        for nn in range(2):
            kb.step("vector", lambda e, nn=nn: e.scalar_tensor_tensor(
                out=res_t[:, nn * 512:(nn + 1) * 512], in0=x_t[:, nn * 512:(nn + 1) * 512],
                scalar=2.0, in1=ps_o[nn][:, :], op0=ALU.mult, op1=ALU.add),
                waits=[MM, LD], inc=EV)
        emit_ln_tile(kb, res_t, D, scr, cols, VD, SD,
                     out_f32=x1_sb[:, m, :], out_bf=x1bf,
                     waits=[EV])
        kb.dma("sync", x1_out[m * P:(m + 1) * P, :], x1_sb[:, m, :],
               waits=[VD], inc=ST)

    for c in range(NCHK):
        cs = slice(c * 512, (c + 1) * 512)
        for tt in range(4):
            m = c * 4 + tt
            kb.step("vector", lambda e, m=m: e.tensor_copy(
                out=x1bf[:, :], in_=x1_sb[:, m, :]), waits=[VD, SD, TR], inc=VD)
            for f in range(8):
                slot = f % 2

                def tr(e, f=f, slot=slot):
                    return e.transpose(ps_tr[slot][:, :], x1bf[:, f * P:(f + 1) * P],
                                       id_sb[:, :])
                kb.step("tensor", tr, waits=[VD, EV], inc=TR)

                def trev(e, f=f, tt=tt, slot=slot):
                    return e.tensor_copy(out=x1fm[:, f, tt * P:(tt + 1) * P],
                                         in_=ps_tr[slot][:, :])
                kb.step("vector", trev, waits=[TR], inc=EV)

        def gmm(e):
            last = None
            for k in range(8):
                last = e.matmul(ps_g[:, :], wr_sb[:, k, :], x1fm[:, k, :],
                                start=(k == 0), stop=(k == 7))
            return last
        kb.step("tensor", gmm, waits=[EV, VD], inc=MM)
        kb.step("vector", lambda e: e.tensor_copy(out=glog_cm[:, :], in_=ps_g[0:8, :]),
                waits=[MM], inc=EV)
        for tt in range(4):
            m = c * 4 + tt
            slot = tt % 2

            def gtr(e, tt=tt, slot=slot):
                return e.transpose(ps_tr[slot][:, 0:8],
                                   glog_cm[0:8, tt * P:(tt + 1) * P],
                                   id_sb[0:8, 0:8])
            kb.step("tensor", gtr, waits=[EV, VD], inc=TR)
            kb.step("vector", lambda e, slot=slot: e.tensor_copy(
                out=glog[:, 0:8], in_=ps_tr[slot][:, 0:8]),
                waits=[TR], inc=EV)
            emit_gate_top2(kb, glog, gcol, gtmp, comb[:, m, 0:NE], VD, SD,
                           waits=[EV])

        for e_i in range(NE):
            for k in range(8):
                kb.dma("gpsimd", w13[:, k, :], w1_in[e_i, k * P:(k + 1) * P, :],
                       waits=[MM], inc=LD)
            for k in range(12):
                kb.dma("gpsimd", wbuf[:, k, :], w2_in[e_i, k * P:(k + 1) * P, :],
                       waits=[MM], inc=LD)
            for hm in range(12):
                def hmm(e, hm=hm, cs=cs):
                    last = None
                    for k in range(8):
                        last = e.matmul(ps_h[:, :],
                                        w13[:, k, hm * P:(hm + 1) * P],
                                        x1fm[:, k, :],
                                        start=(k == 0), stop=(k == 7))
                    return last
                kb.step("tensor", hmm, waits=[LD, EV], inc=MM)
                kb.step("scalar", lambda e, hm=hm: e.activation(
                    out=hsil[:, hm, :], in_=ps_h[:, :], func=AF.Silu),
                    waits=[MM], inc=EV)
            for k in range(8):
                kb.dma("gpsimd", w13[:, k, :], w3_in[e_i, k * P:(k + 1) * P, :],
                       waits=[MM], inc=LD)
            for hm in range(12):
                def gmm2(e, hm=hm, cs=cs):
                    last = None
                    for k in range(8):
                        last = e.matmul(ps_h[:, :],
                                        w13[:, k, hm * P:(hm + 1) * P],
                                        x1fm[:, k, :],
                                        start=(k == 0), stop=(k == 7))
                    return last
                kb.step("tensor", gmm2, waits=[LD, EV], inc=MM)
                kb.step("scalar", lambda e, hm=hm: e.activation(
                    out=gbuf[:, hm, :], in_=ps_h[:, :], func=AF.Copy),
                    waits=[MM], inc=EV)
                kb.step("vector", lambda e, hm=hm: e.tensor_mul(
                    gbuf[:, hm, :], hsil[:, hm, :], gbuf[:, hm, :]),
                    waits=[EV], inc=VD)
            for tt in range(4):
                m = c * 4 + tt

                def ymm(e, tt=tt):
                    last = None
                    for hk in range(12):
                        for nn in range(2):
                            last = e.matmul(ps_y[nn][:, :],
                                            gbuf[:, hk, tt * P:(tt + 1) * P],
                                            wbuf[:, hk, nn * 512:(nn + 1) * 512],
                                            start=(hk == 0), stop=(hk == 11))
                    return last
                kb.step("tensor", ymm, waits=[VD, EV], inc=MM)
                for nn in range(2):
                    if e_i == 0:
                        kb.step("vector", lambda e, tt=tt, m=m, nn=nn, e_i=e_i:
                                e.tensor_scalar(
                                    out=yff[:, tt, nn * 512:(nn + 1) * 512],
                                    in0=ps_y[nn][:, :],
                                    scalar1=comb[:, m, e_i:e_i + 1],
                                    scalar2=None, op0=ALU.mult),
                                waits=[MM], inc=EV)
                    else:
                        kb.step("vector", lambda e, tt=tt, m=m, nn=nn, e_i=e_i:
                                e.scalar_tensor_tensor(
                                    out=yff[:, tt, nn * 512:(nn + 1) * 512],
                                    in0=ps_y[nn][:, :],
                                    scalar=comb[:, m, e_i:e_i + 1],
                                    in1=yff[:, tt, nn * 512:(nn + 1) * 512],
                                    op0=ALU.mult, op1=ALU.add),
                                waits=[MM], inc=EV)
        for tt in range(4):
            m = c * 4 + tt
            kb.step("vector", lambda e, tt=tt, m=m: e.tensor_add(
                res_t[:, :], x1_sb[:, m, :], yff[:, tt, :]),
                waits=[EV, VD, SD], inc=VD)
            emit_ln_tile(kb, res_t, D, scr, cols, VD, SD, out_f32=x_t,
                         waits=[VD, ST])
            kb.dma("sync", x2_out[m * P:(m + 1) * P, :], x_t[:, :],
                   waits=[VD, SD], inc=ST)

    kb.barrier_wait("sync", ST)
    return kb.build()


def prep_launch_C(inp, consts, y_full, x_full):
    in_maps = []
    for k in range(NCORES):
        b, t0 = k // 4, (k % 4) * TOK
        in_maps.append(dict(
            y_in=np.ascontiguousarray(y_full[b, :, t0:t0 + TOK].reshape(16, P, TOK)),
            x_in=np.ascontiguousarray(x_full[b, t0:t0 + TOK]),
            wout=np.asarray(inp["L0_attn_out_proj_w"]),
            wr_in=np.asarray(inp["L0_ffn_Wr"]),
            w1_in=np.asarray(inp["L0_ffn_W1"]),
            w2_in=np.asarray(inp["L0_ffn_W2"]),
            w3_in=np.asarray(inp["L0_ffn_W3"]),
            ident=consts["ident_bf"],
        ))
    return in_maps


def emit_colop(kb, fn, VD, waits=()):
    kb.step("vector", fn, waits=list(waits) + [VD], inc=VD)


def emit_gate_l1(kb, glog, gexp, gorig, gtmp, gp4, gcol, gbb_sb, comb_slice,
                 VD, SD, waits=()):
    E = 16
    w = list(waits)
    kb.step("vector", lambda e: e.reduce_max(gcol[:, 0:1], glog[:, 0:E], axis=AX.X),
            waits=w + [VD, SD], inc=VD)
    emit_colop(kb, lambda e: e.tensor_scalar(out=gcol[:, 1:2], in0=gcol[:, 0:1],
                                             scalar1=-1.0, scalar2=None,
                                             op0=ALU.mult), VD)
    kb.step("scalar", lambda e: e.activation(out=gexp[:, 0:E], in_=glog[:, 0:E],
                                             func=AF.Exp, bias=gcol[:, 1:2]),
            waits=[VD], inc=SD)
    kb.step("vector", lambda e: e.reduce_sum(gcol[:, 2:3], gexp[:, 0:E], axis=AX.X),
            waits=[SD], inc=VD)
    emit_colop(kb, lambda e: e.reciprocal(gcol[:, 3:4], gcol[:, 2:3]), VD)
    emit_colop(kb, lambda e: e.tensor_scalar(out=gorig[:, 0:E], in0=gexp[:, 0:E],
                                             scalar1=gcol[:, 3:4], scalar2=None,
                                             op0=ALU.mult), VD)
    emit_colop(kb, lambda e: e.tensor_add(gtmp[:, 0:E], gorig[:, 0:E],
                                          gbb_sb[:, 0:E]), VD)
    pairs = [(0, 1), (0, 2), (0, 3), (1, 2), (1, 3), (2, 3)]
    first = True
    for (j1, j2) in pairs:
        emit_colop(kb, lambda e, j1=j1, j2=j2: e.tensor_add(
            gp4[:, 8:12], gtmp[:, j1:E:4], gtmp[:, j2:E:4]), VD)
        if first:
            emit_colop(kb, lambda e: e.tensor_copy(out=gp4[:, 0:4],
                                                   in_=gp4[:, 8:12]), VD)
            first = False
        else:
            emit_colop(kb, lambda e: e.tensor_max(gp4[:, 0:4], gp4[:, 0:4],
                                                  gp4[:, 8:12]), VD)
    gpairs = [(0, 1), (0, 2), (0, 3), (1, 2), (1, 3), (2, 3)]
    first = True
    for (j1, j2) in gpairs:
        emit_colop(kb, lambda e, j1=j1, j2=j2: e.tensor_tensor(
            out=gcol[:, 5:6], in0=gp4[:, j1:j1 + 1], in1=gp4[:, j2:j2 + 1],
            op=ALU.min), VD)
        if first:
            emit_colop(kb, lambda e: e.tensor_copy(out=gcol[:, 4:5],
                                                   in_=gcol[:, 5:6]), VD)
            first = False
        else:
            emit_colop(kb, lambda e: e.tensor_max(gcol[:, 4:5], gcol[:, 4:5],
                                                  gcol[:, 5:6]), VD)
    emit_colop(kb, lambda e: e.tensor_scalar(out=gp4[:, 4:8], in0=gp4[:, 0:4],
                                             scalar1=gcol[:, 4:5], scalar2=None,
                                             op0=ALU.is_ge), VD)
    emit_colop(kb, lambda e: e.tensor_scalar(out=gp4[:, 4:8], in0=gp4[:, 4:8],
                                             scalar1=BIG, scalar2=-BIG,
                                             op0=ALU.mult, op1=ALU.add), VD)
    for j in range(4):
        emit_colop(kb, lambda e, j=j: e.tensor_copy(out=gexp[:, j:E:4],
                                                    in_=gp4[:, 4:8]), VD)
    emit_colop(kb, lambda e: e.tensor_add(gtmp[:, 0:E], gtmp[:, 0:E],
                                          gexp[:, 0:E]), VD)
    kb.step("vector", lambda e: e.reduce_max(gcol[:, 6:7], gtmp[:, 0:E], axis=AX.X),
            waits=[VD], inc=VD)
    emit_colop(kb, lambda e: e.tensor_scalar(out=gexp[:, 0:E], in0=gtmp[:, 0:E],
                                             scalar1=gcol[:, 6:7], scalar2=None,
                                             op0=ALU.is_equal), VD)
    emit_colop(kb, lambda e: e.scalar_tensor_tensor(out=gtmp[:, 0:E],
                                                    in0=gexp[:, 0:E], scalar=-BIG,
                                                    in1=gtmp[:, 0:E],
                                                    op0=ALU.mult, op1=ALU.add), VD)
    kb.step("vector", lambda e: e.reduce_max(gcol[:, 7:8], gtmp[:, 0:E], axis=AX.X),
            waits=[VD], inc=VD)
    emit_colop(kb, lambda e: e.tensor_scalar(out=gtmp[:, 0:E], in0=gtmp[:, 0:E],
                                             scalar1=gcol[:, 7:8], scalar2=None,
                                             op0=ALU.is_equal), VD)
    emit_colop(kb, lambda e: e.tensor_add(gexp[:, 0:E], gexp[:, 0:E],
                                          gtmp[:, 0:E]), VD)
    emit_colop(kb, lambda e: e.tensor_mul(comb_slice, gexp[:, 0:E],
                                          gorig[:, 0:E]), VD)


def build_launch_F():
    kb = KB("F")
    NE, INTER, NSH = 16, 512, 1024
    NTT = TOK // P
    NCHK = 4

    y_in = kb.din("y_in", [16, P, TOK], BF16)
    x_in = kb.din("x_in", [TOK, D], F32)
    wout = kb.din("wout", [DIN, D], F32)
    gw_in = kb.din("gw_in", [D, NE], F32)
    gb_in = kb.din("gb_in", [P, NE], F32)
    we1_in = kb.din("we1_in", [NE, D, INTER], F32)
    we2_in = kb.din("we2_in", [NE, INTER, D], F32)
    we3_in = kb.din("we3_in", [NE, D, INTER], F32)
    be1_in = kb.din("be1_in", [NE, P, 4], F32)
    be3_in = kb.din("be3_in", [NE, P, 4], F32)
    be2_in = kb.din("be2_in", [NE, P, D], F32)
    ws1_in = kb.din("ws1_in", [D, NSH], F32)
    ws2_in = kb.din("ws2_in", [NSH, D], F32)
    ws3_in = kb.din("ws3_in", [D, NSH], F32)
    bs1_in = kb.din("bs1_in", [P, 8], F32)
    bs3_in = kb.din("bs3_in", [P, 8], F32)
    bs2_in = kb.din("bs2_in", [P, D], F32)
    ident = kb.din("ident", [P, P], BF16)

    x3_out = kb.dout("x3_out", [TOK, D], F32)
    x4_out = kb.dout("x4_out", [TOK, D], F32)

    wbuf = kb.sbuf("wbuf", [P, 16, D], BF16)
    w13 = kb.sbuf("w13", [P, 8, NSH], BF16)
    gw_sb = kb.sbuf("gw_sb", [P, 8, NE], BF16)
    gbb_sb = kb.sbuf("gbb_sb", [P, NE], F32)
    id_sb = kb.sbuf("id_sb", [P, P], BF16)
    y_blk = kb.sbuf("y_blk", [P, 16, P], BF16)
    x_t = kb.sbuf("x_t", [P, D], F32)
    res_t = kb.sbuf("res_t", [P, D], F32)
    scr = kb.sbuf("scr", [P, D], F32)
    cols = kb.sbuf("cols", [P, 8], F32)
    x3_sb = kb.sbuf("x3_sb", [P, NTT, D], F32)
    x3fm = kb.sbuf("x3fm", [P, 8, 512], BF16)
    x3bf = kb.sbuf("x3bf", [P, D], BF16)
    glog_cm = kb.sbuf("glog_cm", [NE, 512], BF16)
    glog = kb.sbuf("glog", [P, NE], F32)
    gexp = kb.sbuf("gexp", [P, NE], F32)
    gorig = kb.sbuf("gorig", [P, NE], F32)
    gtmp = kb.sbuf("gtmp", [P, NE], F32)
    gp4 = kb.sbuf("gp4", [P, 12], F32)
    gcol = kb.sbuf("gcol", [P, 8], F32)
    comb = kb.sbuf("comb", [P, NTT, NE], F32)
    hsil = kb.sbuf("hsil", [P, 8, 512], BF16)
    gbuf = kb.sbuf("gbuf", [P, 8, 512], BF16)
    yff = kb.sbuf("yff", [P, 4, D], F32)
    be2_sb = kb.sbuf("be2_sb", [P, D], F32)
    bsml = kb.sbuf("bsml", [P, 2 * NE * 4 + 16 + D], F32)

    ps_o = [kb.psum(f"ps_o{i}", [P, 512], F32) for i in range(2)]
    ps_tr = [kb.psum(f"ps_tr{i}", [P, P], BF16) for i in range(2)]
    ps_g = kb.psum("ps_g", [NE, 512], F32)
    ps_h = kb.psum("ps_h", [P, 512], F32)
    ps_y = [kb.psum(f"ps_y{i}", [P, 512], F32) for i in range(2)]

    LD = kb.sem("LD")
    VD = kb.sem("VD")
    SD = kb.sem("SD")
    TR = kb.sem("TR")
    MM = kb.sem("MM")
    EV = kb.sem("EV")
    ST = kb.sem("ST")

    kb.setup_consts([EPS, 1.0, -1.0], LD)

    for a in range(16):
        kb.dma("gpsimd", wbuf[:, a, :], wout[a * P:(a + 1) * P, :], inc=LD)
    for k in range(8):
        kb.dma("gpsimd", gw_sb[:, k, :], gw_in[k * P:(k + 1) * P, :], inc=LD)
    kb.dma("sync", gbb_sb[:, :], gb_in[:, :], inc=LD)
    kb.dma("sync", id_sb[:, :], ident[:, :], inc=LD)
    for e_i in range(NE):
        kb.dma("sync", bsml[:, e_i * 4:(e_i + 1) * 4], be1_in[e_i, :, :], inc=LD)
        kb.dma("sync", bsml[:, 64 + e_i * 4:64 + (e_i + 1) * 4], be3_in[e_i, :, :],
               inc=LD)
    kb.dma("sync", bsml[:, 128:136], bs1_in[:, :], inc=LD)
    kb.dma("sync", bsml[:, 136:144], bs3_in[:, :], inc=LD)
    kb.dma("sync", bsml[:, 144:144 + D], bs2_in[:, :], inc=LD)
    static_ld = LD.v

    for m in range(NTT):
        for a in range(16):
            kb.dma("sync", y_blk[:, a, :], y_in[a, :, m * P:(m + 1) * P],
                   waits=[(LD, static_ld), MM], inc=LD)
        kb.dma("sync", x_t[:, :], x_in[m * P:(m + 1) * P, :], waits=[VD], inc=LD)

        def mmop(e):
            last = None
            for a in range(16):
                for nn in range(2):
                    last = e.matmul(ps_o[nn][:, :], y_blk[:, a, :],
                                    wbuf[:, a, nn * 512:(nn + 1) * 512],
                                    start=(a == 0), stop=(a == 15))
            return last
        kb.step("tensor", mmop, waits=[LD, EV], inc=MM)
        for nn in range(2):
            kb.step("vector", lambda e, nn=nn: e.scalar_tensor_tensor(
                out=res_t[:, nn * 512:(nn + 1) * 512],
                in0=x_t[:, nn * 512:(nn + 1) * 512],
                scalar=1.0, in1=ps_o[nn][:, :], op0=ALU.mult, op1=ALU.add),
                waits=[MM, LD], inc=EV)
        emit_ln_tile(kb, res_t, D, scr, cols, VD, SD,
                     out_f32=x3_sb[:, m, :],
                     waits=[EV])
        kb.dma("sync", x3_out[m * P:(m + 1) * P, :], x3_sb[:, m, :],
               waits=[VD], inc=ST)

    for c in range(NCHK):
        for tt in range(4):
            m = c * 4 + tt
            kb.step("vector", lambda e, m=m: e.tensor_copy(
                out=x3bf[:, :], in_=x3_sb[:, m, :]), waits=[VD, SD, TR], inc=VD)
            for f in range(8):
                slot = f % 2

                def tr(e, f=f, slot=slot):
                    return e.transpose(ps_tr[slot][:, :], x3bf[:, f * P:(f + 1) * P],
                                       id_sb[:, :])
                kb.step("tensor", tr, waits=[VD, EV], inc=TR)

                def trev(e, f=f, tt=tt, slot=slot):
                    return e.tensor_copy(out=x3fm[:, f, tt * P:(tt + 1) * P],
                                         in_=ps_tr[slot][:, :])
                kb.step("vector", trev, waits=[TR], inc=EV)

        def gmm(e):
            last = None
            for k in range(8):
                last = e.matmul(ps_g[:, :], gw_sb[:, k, :], x3fm[:, k, :],
                                start=(k == 0), stop=(k == 7))
            return last
        kb.step("tensor", gmm, waits=[EV, VD], inc=MM)
        kb.step("vector", lambda e: e.tensor_copy(out=glog_cm[:, :],
                                                  in_=ps_g[0:NE, :]),
                waits=[MM], inc=EV)
        for tt in range(4):
            m = c * 4 + tt
            slot = tt % 2

            def gtr(e, tt=tt, slot=slot):
                return e.transpose(ps_tr[slot][:, 0:NE],
                                   glog_cm[0:NE, tt * P:(tt + 1) * P],
                                   id_sb[0:NE, 0:NE])
            kb.step("tensor", gtr, waits=[EV, VD], inc=TR)
            kb.step("vector", lambda e, slot=slot: e.tensor_copy(
                out=glog[:, 0:NE], in_=ps_tr[slot][:, 0:NE]),
                waits=[TR], inc=EV)
            emit_gate_l1(kb, glog, gexp, gorig, gtmp, gp4, gcol, gbb_sb,
                         comb[:, m, 0:NE], VD, SD, waits=[EV])

        for e_i in range(NE):
            for k in range(8):
                kb.dma("gpsimd", w13[:, k, 0:INTER],
                       we1_in[e_i, k * P:(k + 1) * P, :], waits=[MM], inc=LD)
            for k in range(4):
                kb.dma("gpsimd", wbuf[:, k, :], we2_in[e_i, k * P:(k + 1) * P, :],
                       waits=[MM], inc=LD)
            kb.dma("sync", be2_sb[:, :], be2_in[e_i, :, :], waits=[VD], inc=LD)
            for hm in range(4):
                def hmm(e, hm=hm):
                    last = None
                    for k in range(8):
                        last = e.matmul(ps_h[:, :],
                                        w13[:, k, hm * P:(hm + 1) * P],
                                        x3fm[:, k, :],
                                        start=(k == 0), stop=(k == 7))
                    return last
                kb.step("tensor", hmm, waits=[LD, EV], inc=MM)
                kb.step("scalar", lambda e, hm=hm, e_i=e_i: e.activation(
                    out=hsil[:, hm, :], in_=ps_h[:, :], func=AF.Silu,
                    bias=bsml[:, e_i * 4 + hm:e_i * 4 + hm + 1]),
                    waits=[MM], inc=EV)
            for k in range(8):
                kb.dma("gpsimd", w13[:, k, 0:INTER],
                       we3_in[e_i, k * P:(k + 1) * P, :], waits=[MM], inc=LD)
            for hm in range(4):
                def gmm2(e, hm=hm):
                    last = None
                    for k in range(8):
                        last = e.matmul(ps_h[:, :],
                                        w13[:, k, hm * P:(hm + 1) * P],
                                        x3fm[:, k, :],
                                        start=(k == 0), stop=(k == 7))
                    return last
                kb.step("tensor", gmm2, waits=[LD, EV], inc=MM)
                kb.step("vector", lambda e, hm=hm, e_i=e_i: e.tensor_scalar(
                    out=gbuf[:, hm, :], in0=ps_h[:, :],
                    scalar1=bsml[:, 64 + e_i * 4 + hm:64 + e_i * 4 + hm + 1],
                    scalar2=None, op0=ALU.add),
                    waits=[MM, VD], inc=EV)
                kb.step("vector", lambda e, hm=hm: e.tensor_mul(
                    gbuf[:, hm, :], hsil[:, hm, :], gbuf[:, hm, :]),
                    waits=[EV], inc=VD)
            for tt in range(4):
                m = c * 4 + tt

                def ymm(e, tt=tt):
                    last = None
                    for hk in range(4):
                        for nn in range(2):
                            last = e.matmul(ps_y[nn][:, :],
                                            gbuf[:, hk, tt * P:(tt + 1) * P],
                                            wbuf[:, hk, nn * 512:(nn + 1) * 512],
                                            start=(hk == 0), stop=(hk == 3))
                    return last
                kb.step("tensor", ymm, waits=[VD, EV], inc=MM)
                for nn in range(2):
                    sl = slice(nn * 512, (nn + 1) * 512)
                    if e_i == 0:
                        kb.step("vector", lambda e, tt=tt, m=m, nn=nn, sl=sl, e_i=e_i:
                                e.tensor_scalar(
                                    out=yff[:, tt, sl], in0=ps_y[nn][:, :],
                                    scalar1=comb[:, m, e_i:e_i + 1],
                                    scalar2=None, op0=ALU.mult),
                                waits=[MM], inc=EV)
                    else:
                        kb.step("vector", lambda e, tt=tt, m=m, nn=nn, sl=sl, e_i=e_i:
                                e.scalar_tensor_tensor(
                                    out=yff[:, tt, sl], in0=ps_y[nn][:, :],
                                    scalar=comb[:, m, e_i:e_i + 1],
                                    in1=yff[:, tt, sl],
                                    op0=ALU.mult, op1=ALU.add),
                                waits=[MM], inc=EV)
                    kb.step("vector", lambda e, tt=tt, m=m, sl=sl, e_i=e_i:
                            e.scalar_tensor_tensor(
                                out=yff[:, tt, sl], in0=be2_sb[:, sl],
                                scalar=comb[:, m, e_i:e_i + 1],
                                in1=yff[:, tt, sl],
                                op0=ALU.mult, op1=ALU.add),
                            waits=[EV, LD], inc=VD)
        for k in range(8):
            kb.dma("gpsimd", w13[:, k, :], ws1_in[k * P:(k + 1) * P, :],
                   waits=[MM], inc=LD)
        for hm in range(8):
            def hmms(e, hm=hm):
                last = None
                for k in range(8):
                    last = e.matmul(ps_h[:, :], w13[:, k, hm * P:(hm + 1) * P],
                                    x3fm[:, k, :], start=(k == 0), stop=(k == 7))
                return last
            kb.step("tensor", hmms, waits=[LD, EV], inc=MM)
            kb.step("scalar", lambda e, hm=hm: e.activation(
                out=hsil[:, hm, :], in_=ps_h[:, :], func=AF.Silu,
                bias=bsml[:, 128 + hm:128 + hm + 1]),
                waits=[MM], inc=EV)
        for k in range(8):
            kb.dma("gpsimd", w13[:, k, :], ws3_in[k * P:(k + 1) * P, :],
                   waits=[MM], inc=LD)
        for hm in range(8):
            def gmms(e, hm=hm):
                last = None
                for k in range(8):
                    last = e.matmul(ps_h[:, :], w13[:, k, hm * P:(hm + 1) * P],
                                    x3fm[:, k, :], start=(k == 0), stop=(k == 7))
                return last
            kb.step("tensor", gmms, waits=[LD, EV], inc=MM)
            kb.step("vector", lambda e, hm=hm: e.tensor_scalar(
                out=gbuf[:, hm, :], in0=ps_h[:, :],
                scalar1=bsml[:, 136 + hm:136 + hm + 1], scalar2=None, op0=ALU.add),
                waits=[MM, VD], inc=EV)
            kb.step("vector", lambda e, hm=hm: e.tensor_mul(
                gbuf[:, hm, :], hsil[:, hm, :], gbuf[:, hm, :]),
                waits=[EV], inc=VD)
        for k in range(8):
            kb.dma("gpsimd", wbuf[:, k, :], ws2_in[k * P:(k + 1) * P, :],
                   waits=[MM], inc=LD)
        for tt in range(4):
            def ymms(e, tt=tt):
                last = None
                for hk in range(8):
                    for nn in range(2):
                        last = e.matmul(ps_y[nn][:, :],
                                        gbuf[:, hk, tt * P:(tt + 1) * P],
                                        wbuf[:, hk, nn * 512:(nn + 1) * 512],
                                        start=(hk == 0), stop=(hk == 7))
                return last
            kb.step("tensor", ymms, waits=[VD, EV, LD], inc=MM)
            for nn in range(2):
                sl = slice(nn * 512, (nn + 1) * 512)
                kb.step("vector", lambda e, tt=tt, sl=sl, nn=nn:
                        e.scalar_tensor_tensor(
                            out=yff[:, tt, sl], in0=ps_y[nn][:, :], scalar=1.0,
                            in1=yff[:, tt, sl], op0=ALU.mult, op1=ALU.add),
                        waits=[MM, VD], inc=EV)
            kb.step("vector", lambda e, tt=tt: e.tensor_add(
                yff[:, tt, :], yff[:, tt, :], bsml[:, 144:144 + D]),
                waits=[EV], inc=VD)
        for tt in range(4):
            m = c * 4 + tt
            kb.step("vector", lambda e, tt=tt, m=m: e.tensor_add(
                res_t[:, :], x3_sb[:, m, :], yff[:, tt, :]),
                waits=[EV, VD, SD], inc=VD)
            emit_ln_tile(kb, res_t, D, scr, cols, VD, SD, out_f32=x_t,
                         waits=[VD, ST])
            kb.dma("sync", x4_out[m * P:(m + 1) * P, :], x_t[:, :],
                   waits=[VD, SD], inc=ST)

    kb.barrier_wait("sync", ST)
    return kb.build()


def prep_launch_F(inp, consts, y_full, x2_full):
    woutT = np.ascontiguousarray(np.asarray(inp["L1_attn_out_proj_w"]).T)
    gwT = np.ascontiguousarray(np.asarray(inp["L1_ffn_gate_w"]).T)
    gb_bc = np.tile(np.asarray(inp["L1_ffn_gate_b"])[None, :], (P, 1)).astype(np.float32)
    be1 = np.ascontiguousarray(
        np.asarray(inp["L1_ffn_be1"]).reshape(16, 4, P).transpose(0, 2, 1))
    be3 = np.ascontiguousarray(
        np.asarray(inp["L1_ffn_be3"]).reshape(16, 4, P).transpose(0, 2, 1))
    be2 = np.ascontiguousarray(
        np.broadcast_to(np.asarray(inp["L1_ffn_be2"])[:, None, :], (16, P, D)))
    bs1 = np.ascontiguousarray(np.asarray(inp["L1_ffn_bs1"]).reshape(8, P).T)
    bs3 = np.ascontiguousarray(np.asarray(inp["L1_ffn_bs3"]).reshape(8, P).T)
    bs2 = np.tile(np.asarray(inp["L1_ffn_bs2"])[None, :], (P, 1)).astype(np.float32)
    in_maps = []
    for k in range(NCORES):
        b, t0 = k // 4, (k % 4) * TOK
        in_maps.append(dict(
            y_in=np.ascontiguousarray(y_full[b, :, t0:t0 + TOK].reshape(16, P, TOK)),
            x_in=np.ascontiguousarray(x2_full[b, t0:t0 + TOK]),
            wout=woutT, gw_in=gwT, gb_in=gb_bc,
            we1_in=np.asarray(inp["L1_ffn_We1"]),
            we2_in=np.asarray(inp["L1_ffn_We2"]),
            we3_in=np.asarray(inp["L1_ffn_We3"]),
            be1_in=be1, be3_in=be3, be2_in=be2,
            ws1_in=np.asarray(inp["L1_ffn_ws1"]),
            ws2_in=np.asarray(inp["L1_ffn_ws2"]),
            ws3_in=np.asarray(inp["L1_ffn_ws3"]),
            bs1_in=bs1, bs3_in=bs3, bs2_in=bs2,
            ident=consts["ident_bf"],
        ))
    return in_maps


def make_consts():
    import ml_dtypes
    return dict(ident_bf=np.eye(P, dtype=np.float32).astype(ml_dtypes.bfloat16))


def assemble_ch_major(res, key):
    out = np.zeros((B, DIN, L), dtype=res[0][key].dtype)
    for k in range(NCORES):
        b, t0 = k // 4, (k % 4) * TOK
        out[b, :, t0:t0 + TOK] = res[k][key].reshape(DIN, TOK)
    return out


def assemble_rows(res, key, width=D):
    out = np.zeros((B, L, width), dtype=res[0][key].dtype)
    for k in range(NCORES):
        b, t0 = k // 4, (k % 4) * TOK
        out[b, t0:t0 + TOK] = res[k][key]
    return out


_BUILD_CACHE = {}


def _get_nc(name, builder):
    if name not in _BUILD_CACHE:
        _BUILD_CACHE[name] = builder()
    return _BUILD_CACHE[name]


def _assemble_bc(res, key):
    out = np.zeros((B, 32, L), dtype=res[0][key].dtype)
    for k in range(NCORES):
        b, t0 = k // 4, (k % 4) * TOK
        out[b, :, t0:t0 + TOK] = res[k][key]
    return out


def kernel(**inputs):
    consts = make_consts()
    cores = list(range(NCORES))

    nc = _get_nc("A", build_launch_A)
    res = run_bass_kernel_spmd(nc, prep_launch_A(inputs, consts), core_ids=cores).results
    u0 = assemble_ch_major(res, "u_out")
    dt0 = assemble_ch_major(res, "dt_out")
    zs0 = assemble_ch_major(res, "zs_out")
    bc0 = _assemble_bc(res, "bc_out")

    nc = _get_nc("S1", lambda: build_launch_scan(ndir=1))
    dirs = [dict(dt=dt0, u=u0, Bm=bc0[:, 0:16], Cm=bc0[:, 16:32],
                 A_log=inputs["L0_attn_A_log"], D=inputs["L0_attn_D_skip"])]
    res = run_bass_kernel_spmd(nc, prep_launch_scan(dirs, zs0), core_ids=cores).results
    y0 = assemble_scan_y(res)

    nc = _get_nc("C", build_launch_C)
    res = run_bass_kernel_spmd(nc, prep_launch_C(inputs, consts, y0,
                                                 np.asarray(inputs["L0_attn_x"])),
                               core_ids=cores).results
    x2 = assemble_rows(res, "x2_out")

    nc = _get_nc("D", build_launch_D)
    res = run_bass_kernel_spmd(nc, prep_launch_D(inputs, consts, x2),
                               core_ids=cores).results
    uf = assemble_ch_major(res, "u_f_out")
    dtf = assemble_ch_major(res, "dt_f_out")
    ub = assemble_ch_major(res, "u_b_out")
    dtb = assemble_ch_major(res, "dt_b_out")
    zs1 = assemble_ch_major(res, "zs_out")
    bcf = _assemble_bc(res, "bc_f_out")
    bcb = _assemble_bc(res, "bc_b_out")

    nc = _get_nc("S2", lambda: build_launch_scan(ndir=2))
    dirs = [dict(dt=dtf, u=uf, Bm=bcf[:, 0:16], Cm=bcf[:, 16:32],
                 A_log=inputs["L1_attn_A_log"], D=inputs["L1_attn_D"]),
            dict(dt=dtb, u=ub, Bm=bcb[:, 0:16], Cm=bcb[:, 16:32],
                 A_log=inputs["L1_attn_A_b_log"], D=inputs["L1_attn_D_b"])]
    res = run_bass_kernel_spmd(nc, prep_launch_scan(dirs, zs1), core_ids=cores).results
    y1 = assemble_scan_y(res)

    nc = _get_nc("F", build_launch_F)
    res = run_bass_kernel_spmd(nc, prep_launch_F(inputs, consts, y1, x2),
                               core_ids=cores).results
    x4 = assemble_rows(res, "x4_out")
    return x4.astype(np.float32)
